# version 02
# baseline (speedup 1.0000x reference)
import numpy as np
import ml_dtypes
import concourse.bass as bass
import concourse.mybir as mybir
from concourse.bass_utils import run_bass_kernel_spmd
from contextlib import ExitStack

F32 = mybir.dt.float32
BF16 = mybir.dt.bfloat16
AF = mybir.ActivationFunctionType
ALU = mybir.AluOpType
AX = mybir.AxisListType

S = 2048
DM = 2048
DFF = 8192
NE = 8
NT = S // 128
ND = DM // 128
HALF = S // 2
NTH = NT // 2
EPS_RMS = 1.1920929e-07
EPS_LN = 1e-5
N_CORES = 8
SB_BASE = 20448
SB_LIMIT = 229344


class Sem:
    __slots__ = ("h", "n")

    def __init__(self, nc, name):
        self.h = nc.alloc_semaphore(name)
        self.n = 0

    def inc(self, instr, k=1):
        instr.then_inc(self.h, k)
        self.n += k
        return instr


def build_program(stages=("ffn1", "attn", "proj", "ffn2")):
    nc = bass.Bass("TRN2", target_bir_lowering=False)
    d = nc.dram_tensor
    x_in = d("x", [S, DM], F32, kind="ExternalInput").ap()
    wqkvT_p2 = d("wqkvT_p2", [48, DM, 128], F32, kind="ExternalInput").ap()
    wqkT_p1 = d("wqkT_p1", [DM, 2 * DM], F32, kind="ExternalInput").ap()
    wuT_blk = d("wuT_blk", [64, DM, 128], F32, kind="ExternalInput").ap()
    wgT_blk = d("wgT_blk", [64, DM, 128], F32, kind="ExternalInput").ap()
    woutT = d("woutT", [DFF, DM], F32, kind="ExternalInput").ap()
    gateT = d("gateT", [DM, NE], F32, kind="ExternalInput").ap()
    expb = d("expb", [NE, DM], F32, kind="ExternalInput").ap()
    expT_blk = d("expT_blk", [NE, 4, DM, 512], F32, kind="ExternalInput").ap()
    y_out = d("y", [S, DM], F32, kind="ExternalOutput").ap()
    xspill = d("xspill", [S, DM], F32, kind="Internal").ap()
    scr_r = d("scr_r", [2, NT, 128], F32, kind="Internal").ap()

    c_ident_bf = nc.inline_tensor(np.eye(128, dtype=ml_dtypes.bfloat16), "c_ib").ap()
    c_ident_f = nc.inline_tensor(np.eye(128, dtype=np.float32), "c_if").ap()
    c_eps = nc.inline_tensor(
        np.tile(np.array([[EPS_LN, EPS_RMS, 1e-6, 0.0]], dtype=np.float32), (128, 1)),
        "c_eps").ap()

    def at(name, shape, dtype, offset):
        offset += SB_BASE
        per = int(np.prod(shape[1:])) * mybir.dt.size(dtype)
        assert offset + per <= SB_LIMIT, f"sbuf overflow {name}: {offset + per}"
        return nc.alloc_sbuf_tensor_at(name, list(shape), dtype, offset=offset)

    o = [0]

    def nxt(name, shape, dtype):
        per = int(np.prod(shape[1:])) * mybir.dt.size(dtype)
        off = (o[0] + 31) // 32 * 32
        t = at(name, shape, dtype, off)
        o[0] = off + per
        return t

    ident_bf = nxt("ident_bf", [128, 128], BF16)
    ident_f = nxt("ident_f", [128, 128], F32)
    ones1_f = nxt("ones1_f", [1, 128], F32)
    gateT_bf = nxt("gateT_bf", [128, ND * NE], BF16)
    expb_bf = nxt("expb_bf", [NE, DM], BF16)
    wmaskT = nxt("wmaskT", [NE, S], F32)
    wmaskT_bf = nxt("wmaskT_bf", [NE, S], BF16)
    wm = nxt("wm", [128, NTH * NE], F32)
    gsc = nxt("gsc", [128, 32], F32)
    lnsc = nxt("lnsc", [128, 48], F32)
    rq_all = nxt("rq_all", [128, NT], F32)
    rk_all = nxt("rk_all", [128, NT], F32)
    rqkT = nxt("rqkT", [NT, 128], F32)
    ln_tm_off = (o[0] + 31) // 32 * 32
    ln_tm = nxt("ln_tm", [128, DM], BF16)
    ln_f32 = nxt("ln_f32", [128, DM], F32)
    rqh_row = at("rqh_row", [1, HALF], F32, ln_tm_off)
    rkh_row = at("rkh_row", [1, HALF], F32, ln_tm_off + 4096)
    zeros = nxt("zeros", [128, HALF], F32)
    stats_all = nxt("stats_all", [128, NT * 8 * 6], F32)
    rkb = nxt("rkb", [128, HALF], F32)
    ceps = nxt("ceps", [128, 4], F32)
    assert o[0] <= 49152, o[0]

    R1, R2, R2B, R3 = 49152, 114688, 147456, 180224
    xr1 = at("xr1", [128, NTH * DM], F32, R1)
    xrt = [at(f"xrt{i}", [128, DM], F32, R1 + i * 8192) for i in range(2)]
    wqk1 = [at(f"wqk1_{i}", [128, 16 * 512], BF16, R1 + i * 16384) for i in range(2)]
    yT = at("yT", [128, ND * S], BF16, R1)
    xr2 = at("xr2", [128, NTH * DM], F32, R1)
    lnT = at("lnT", [128, ND * S], BF16, R2)
    lnTh = at("lnTh", [128, ND * HALF], BF16, R2)
    wblk = [at(f"wblk{i}", [128, 16 * 512], BF16, R2B + i * 16384) for i in range(2)]
    h_sg = at("h_sg", [128, 16 * HALF], BF16, R2)
    wob = [at(f"wob{i}", [128, 16 * 512], BF16, R2B + i * 16384) for i in range(2)]
    wst = [at(f"wst{i}", [128, ND * 128], BF16, R3 + i * 4096) for i in range(2)]
    at1 = at("at1", [128, HALF], F32, R3 + 8192)
    at2 = at("at2", [128, HALF], F32, R3 + 12288)
    at3 = at("at3", [128, HALF], F32, R3 + 16384)
    at4 = at("at4", [128, HALF], F32, R3 + 20480)
    rqb = at("rqb", [128, HALF], F32, R3 + 24576)
    silu_t = at("silu_t", [128, HALF], F32, R3 + 8192)
    tmpout = [at(f"tmpout{i}", [128, 512], F32, R3 + 12288 + i * 2048) for i in range(2)]

    ctx = ExitStack()
    pb = [ctx.enter_context(nc.psum_tensor(f"pb{i}", [128, 512], F32)) for i in range(8)]

    def pbf(i):
        return pb[i][:].bitcast(BF16)

    sems = {}

    def S_(name):
        if name not in sems:
            sems[name] = Sem(nc, "s_" + name)
        return sems[name]

    s_x = S_("x")
    s_pe = S_("pe")
    s_ve = S_("ve")
    s_se = S_("se")
    s_ev = S_("ev")
    s_sp = S_("sp")
    s_ad = S_("ad")
    s_od = S_("od")
    s_rr = S_("rr")
    adp_sems = [S_("adp0"), S_("adp1")]
    wsets = {}
    for nm in ("A", "B", "C", "D"):
        wsets[nm] = {"ld": [S_("wl" + nm + "0"), S_("wl" + nm + "1")], "i": 0, "done": []}

    def w_stream(g, setname, buf_list, src_ap):
        ws = wsets[setname]
        i = ws["i"]
        buf = buf_list[i % 2]
        if i >= 2:
            sem, val = ws["done"][i - 2]
            g.wait_ge(sem.h, val)
        nblk = src_ap.shape[1]
        ws["ld"][i % 2].inc(g.dma_start(
            out=buf[:].rearrange("p (b i) -> p b i", b=nblk), in_=src_ap), 16)
        ws["i"] = i + 1
        return buf

    def vedge(v, instr):
        s_ve.inc(instr)
        v.wait_ge(s_ve.h, s_ve.n)

    def sedge(e, instr):
        s_se.inc(instr)
        e.wait_ge(s_se.h, s_se.n)

    blk_ctx = nc.Block()
    blk = blk_ctx.__enter__()

    s_bar = S_("bar")
    bar_n = [0]

    def barrier():
        bar_n[0] += 1
        tgt = 5 * bar_n[0]
        for ename in ("sync", "tensor", "vector", "scalar", "gpsimd"):
            def _f(e, tgt=tgt):
                e.drain()
                e.nop().then_inc(s_bar.h, 1)
                e.wait_ge(s_bar.h, tgt)
            getattr(blk, ename)(_f)

    s_cg = S_("cg")

    @blk.sync
    def _(sy):
        s_x.inc(sy.dma_start(out=ident_bf[:], in_=c_ident_bf[:]), 16)
        s_x.inc(sy.dma_start(out=ceps[:], in_=c_eps[:]), 16)
        s_x.inc(sy.dma_start(out=ident_f[:], in_=c_ident_f[:]), 16)
        sy.wait_ge(s_x.h, s_x.n)

    @blk.gpsimd
    def _(g):
        s_cg.inc(g.dma_start(out=gateT_bf[:].rearrange("p (c e) -> p c e", e=NE),
                             in_=gateT.rearrange("(c p) e -> p c e", p=128)), 16)
        s_cg.inc(g.dma_start(out=expb_bf[:], in_=expb), 16)
        g.wait_ge(s_cg.h, s_cg.n)

    @blk.vector
    def _(v):
        v.memset(ones1_f[:], 1.0)
        s_ve.inc(v.memset(zeros[:], 0.0))

    barrier()

    s_lnt = S_("lnt")
    ln2_tm = [at(f"ln2tm{i}", [128, DM], BF16, R3 + i * 4096) for i in range(2)]
    a1f32 = [at(f"a1f{i}", [128, DM], F32, R3 + 8192 + i * 8192) for i in range(2)]

    def emit_ln_batch(srcs, dst_lnT, col_of, rms_fold, accum_rows=None):
        n = len(srcs)
        L = lnsc
        mean_str = L[:, 0:2 * n:2]
        var_str = L[:, 1:2 * n:2]

        @blk.vector
        def _(v):
            for ci in range(n):
                for i in range(4):
                    ins = v.bn_stats(stats_all[:, ci * 24 + i * 6: ci * 24 + (i + 1) * 6],
                                     srcs[ci][:, i * 512:(i + 1) * 512])
            vedge(v, ins)
            for ci in range(n):
                ins = v.bn_aggr(L[:, 2 * ci:2 * ci + 2], stats_all[:, ci * 24:ci * 24 + 24])
            vedge(v, ins)

        @blk.scalar
        def _(e):
            e.wait_ge(s_ve.h, s_ve.n)
            sedge(e, e.activation(L[:, 16:16 + n], var_str, AF.Ln, bias=ceps[:, 0:1]))
            sedge(e, e.activation(L[:, 16:16 + n], L[:, 16:16 + n], AF.Exp, scale=-0.5))

        if rms_fold:
            @blk.vector
            def _(v):
                v.wait_ge(s_se.h, s_se.n)
                vedge(v, v.tensor_tensor(L[:, 40:40 + n], L[:, 16:16 + n],
                                         L[:, 16:16 + n], op=ALU.mult))
                vedge(v, v.tensor_tensor(L[:, 40:40 + n], var_str, L[:, 40:40 + n],
                                         op=ALU.mult))

            @blk.scalar
            def _(e):
                e.wait_ge(s_ve.h, s_ve.n)
                sedge(e, e.activation(L[:, 40:40 + n], L[:, 40:40 + n], AF.Ln,
                                      bias=ceps[:, 1:2]))
                sedge(e, e.activation(L[:, 40:40 + n], L[:, 40:40 + n], AF.Exp, scale=-0.5))

        @blk.vector
        def _(v):
            v.wait_ge(s_se.h, s_se.n)
            if rms_fold:
                vedge(v, v.tensor_tensor(L[:, 24:24 + n], L[:, 16:16 + n], L[:, 40:40 + n],
                                         op=ALU.mult))
            else:
                vedge(v, v.tensor_copy(L[:, 24:24 + n], L[:, 16:16 + n]))
            ins = v.scalar_tensor_tensor(L[:, 32:32 + n], mean_str, -1.0, L[:, 24:24 + n],
                                         op0=ALU.mult, op1=ALU.mult)
            vedge(v, ins)

        affine_ve = [0] * n
        ad_marks = [adp_sems[0].n, adp_sems[1].n]
        width = dst_lnT.shape[1] // ND

        def emit_affine(ci):
            @blk.vector
            def _(v):
                if ci >= 2:
                    v.wait_ge(s_pe.h, tr_done[ci - 2])
                ins = v.tensor_scalar(ln2_tm[ci % 2][:], srcs[ci][:],
                                      L[:, 24 + ci:25 + ci], L[:, 32 + ci:33 + ci],
                                      op0=ALU.mult, op1=ALU.add)
                if accum_rows is None:
                    vedge(v, ins)
                else:
                    s_ve.inc(ins)
                    v.wait_ge(adp_sems[ci % 2].h, ad_marks[ci % 2])
                    ins = v.tensor_scalar(a1f32[ci % 2][:], srcs[ci][:],
                                          L[:, 24 + ci:25 + ci], L[:, 32 + ci:33 + ci],
                                          op0=ALU.mult, op1=ALU.add)
                    vedge(v, ins)
            affine_ve[ci] = s_ve.n

        tr_done = [0] * n
        emit_affine(0)
        for ci in range(n):
            col = col_of(ci)

            @blk.tensor
            def _(t):
                t.wait_ge(s_ve.h, affine_ve[ci])
                t.wait_ge(s_lnt.h, s_lnt.n)
                for h8 in range(2):
                    bank = 6 + h8
                    for i in range(8):
                        dt = h8 * 8 + i
                        ins = t.transpose(pbf(bank)[:, i * 128:(i + 1) * 128],
                                          ln2_tm[ci % 2][:, dt * 128:(dt + 1) * 128],
                                          ident_bf[:])
                    s_pe.inc(ins)
            tr_done[ci] = s_pe.n

            if accum_rows is not None:
                @blk.gpsimd
                def _(g):
                    g.wait_ge(s_ve.h, affine_ve[ci])
                    rows = accum_rows + ci * 128
                    ins = g.dma_start(out=xspill[rows:rows + 128, :],
                                      in_=a1f32[ci % 2][:], accum_op=ALU.add)
                    adp_sems[ci % 2].inc(ins, 16)
                ad_marks[ci % 2] = adp_sems[ci % 2].n

            if ci + 1 < n:
                emit_affine(ci + 1)

            @blk.vector
            def _(v):
                for h8 in range(2):
                    v.wait_ge(s_pe.h, tr_done[ci] - 1 + h8)
                    bank = 6 + h8
                    src = pbf(bank)[:, 0:1024].rearrange("p (b i) -> p b i", b=8)
                    dsts = dst_lnT[:].rearrange("p (dt w) -> p dt w", dt=ND)[
                        :, h8 * 8:(h8 + 1) * 8, col:col + 128]
                    ins = v.tensor_copy(dsts, src)
                s_lnt.inc(ins)
                v.wait_ge(s_lnt.h, s_lnt.n)

    def emit_gating_batch(gbase, lnT_t):
        G = gsc
        glog = stats_all[:, 256:320]
        mk1 = stats_all[:, 320:384]
        mk2 = stats_all[:, 384:448]
        width = lnT_t.shape[1] // ND

        @blk.tensor
        def _(t):
            t.wait_ge(s_lnt.h, s_lnt.n)
            t.wait_ge(s_ev.h, s_ev.n)
            for ci in range(NTH):
                for dt in range(ND):
                    ins = t.matmul(
                        pb[5][:, ci * 8:(ci + 1) * 8],
                        lnT_t[:, dt * width + ci * 128: dt * width + ci * 128 + 128],
                        gateT_bf[:, dt * 8:(dt + 1) * 8],
                        start=(dt == 0), stop=(dt == ND - 1))
            s_pe.inc(ins)

        @blk.vector
        def _(v):
            v.wait_ge(s_pe.h, s_pe.n)
            ins = v.tensor_copy(glog, pb[5][:, 0:64])
            s_ev.inc(ins)
            v.wait_ge(s_ev.h, s_ev.n)
            for ci in range(NTH):
                gl = glog[:, ci * 8:(ci + 1) * 8]
                vedge(v, v.reduce_max(G[:, ci:ci + 1], gl, axis=AX.X))
                vedge(v, v.tensor_scalar(mk1[:, ci * 8:(ci + 1) * 8], gl,
                                         G[:, ci:ci + 1], None, op0=ALU.is_ge))
                vedge(v, v.scalar_tensor_tensor(mk2[:, ci * 8:(ci + 1) * 8],
                                                mk1[:, ci * 8:(ci + 1) * 8], -1e30, gl,
                                                op0=ALU.mult, op1=ALU.add))
                vedge(v, v.reduce_max(G[:, 8 + ci:9 + ci],
                                      mk2[:, ci * 8:(ci + 1) * 8], axis=AX.X))
                vedge(v, v.tensor_scalar(mk2[:, ci * 8:(ci + 1) * 8],
                                         mk2[:, ci * 8:(ci + 1) * 8],
                                         G[:, 8 + ci:9 + ci], None, op0=ALU.is_ge))
            vedge(v, v.tensor_tensor(G[:, 16:24], G[:, 0:8], G[:, 8:16], op=ALU.subtract))

        @blk.scalar
        def _(e):
            e.wait_ge(s_ve.h, s_ve.n)
            sedge(e, e.activation(G[:, 24:32], G[:, 16:24], AF.Exp, scale=-1.0))
            sedge(e, e.activation(G[:, 24:32], G[:, 24:32], AF.Ln, bias=1.0))
            sedge(e, e.activation(G[:, 24:32], G[:, 24:32], AF.Exp, scale=-1.0))

        @blk.vector
        def _(v):
            v.wait_ge(s_se.h, s_se.n)
            vedge(v, v.tensor_scalar_mul(G[:, 24:32], G[:, 24:32], 0.5))
            vedge(v, v.tensor_scalar(G[:, 8:16], G[:, 24:32], -1.0, 0.5,
                                     op0=ALU.mult, op1=ALU.add))
            for ci in range(NTH):
                vedge(v, v.tensor_scalar_mul(mk1[:, ci * 8:(ci + 1) * 8],
                                             mk1[:, ci * 8:(ci + 1) * 8],
                                             G[:, 24 + ci:25 + ci]))
                vedge(v, v.tensor_scalar_mul(mk2[:, ci * 8:(ci + 1) * 8],
                                             mk2[:, ci * 8:(ci + 1) * 8],
                                             G[:, 8 + ci:9 + ci]))
            vedge(v, v.tensor_tensor(wm[:], mk1[:, 0:64], mk2[:, 0:64], op=ALU.add))

        for rnd in range(2):
            @blk.tensor
            def _(t):
                t.wait_ge(s_ve.h, s_ve.n)
                t.wait_ge(s_ev.h, s_ev.n)
                for j in range(4):
                    ci = rnd * 4 + j
                    ins = t.transpose(pb[5][0:8, j * 128:(j + 1) * 128],
                                      wm[:, ci * 8:(ci + 1) * 8], ident_f[:])
                s_pe.inc(ins)

            @blk.vector
            def _(v):
                v.wait_ge(s_pe.h, s_pe.n)
                gc = (gbase + rnd * 4) * 128
                ins = v.tensor_copy(wmaskT[0:8, gc:gc + 512], pb[5][0:8, 0:512])
                s_ev.inc(ins)
                v.wait_ge(s_ev.h, s_ev.n)

        @blk.vector
        def _(v):
            gc = gbase * 128
            vedge(v, v.tensor_copy(wmaskT_bf[0:8, gc:gc + HALF], wmaskT[0:8, gc:gc + HALF]))

    def emit_ffn(ffn_idx, xr, src_dram, do_output):
        for half in range(2):
            gbase = half * NTH
            rows = half * HALF

            @blk.sync
            def _(sy):
                if ffn_idx == 1:
                    sy.wait_ge(s_ad.h, s_ad.n)
                    for sp_ in adp_sems:
                        sy.wait_ge(sp_.h, sp_.n)
                sy.wait_ge(s_od.h, s_od.n)
                sy.wait_ge(s_sp.h, s_sp.n)
                ins = sy.dma_start(
                    out=xr[:].rearrange("p (c d) -> p c d", c=NTH),
                    in_=src_dram[rows:rows + HALF, :].rearrange("(c p) d -> p c d", p=128))
                s_x.inc(ins, 16)

            xload_t = s_x.n

            @blk.vector
            def _(v):
                v.wait_ge(s_x.h, xload_t)

            emit_ln_batch([xr[:, tc * DM:(tc + 1) * DM] for tc in range(NTH)],
                          lnTh, lambda ci: ci * 128, rms_fold=False)
            emit_gating_batch(gbase, lnTh)

            for tc in range(NTH):
                gc = (gbase + tc) * 128

                @blk.tensor
                def _(t):
                    t.wait_ge(s_ve.h, s_ve.n)
                    t.wait_ge(s_ev.h, s_ev.n)
                    for ot in range(4):
                        ins = t.matmul(pb[ot][:], wmaskT_bf[0:8, gc:gc + 128],
                                       expb_bf[0:8, ot * 512:(ot + 1) * 512],
                                       start=True, stop=True)
                        s_pe.inc(ins)

                @blk.vector
                def _(v):
                    v.wait_ge(s_pe.h, s_pe.n)
                    for ot in range(4):
                        dst = xr[:, tc * DM + ot * 512: tc * DM + (ot + 1) * 512]
                        ins = v.tensor_tensor(dst, dst, pb[ot][:], op=ALU.add)
                        s_ev.inc(ins)
                    v.wait_ge(s_ev.h, s_ev.n)

            ev_base = s_ev.n
            nb = 0
            for ie in range(NE):
                for ot in range(4):
                    @blk.gpsimd
                    def _(g):
                        w_stream(g, "A", wblk,
                                 expT_blk[ie, ot].rearrange("(dc p) o -> p dc o", p=128))

                    ld_s = wsets["A"]["ld"][(wsets["A"]["i"] - 1) % 2]
                    ld_t = ld_s.n
                    bi = nb

                    @blk.tensor
                    def _(t):
                        t.wait_ge(ld_s.h, ld_t)
                        if bi == 0:
                            t.wait_ge(s_ev.h, ev_base)
                        for tc in range(NTH):
                            if bi > 0:
                                t.wait_ge(s_ev.h, ev_base + (bi - 1) * NTH + tc + 1)
                            for dt in range(ND):
                                ins = t.matmul(
                                    pb[tc][:],
                                    lnTh[:, dt * HALF + tc * 128: dt * HALF + tc * 128 + 128],
                                    wblk[bi % 2][:, dt * 512:(dt + 1) * 512],
                                    start=(dt == 0), stop=(dt == ND - 1))
                            s_pe.inc(ins)
                    wsets["A"]["done"].append((s_pe, s_pe.n))

                    pe_base = s_pe.n - NTH

                    @blk.vector
                    def _(v):
                        for tc in range(NTH):
                            v.wait_ge(s_pe.h, pe_base + tc + 1)
                            dst = xr[:, tc * DM + ot * 512: tc * DM + (ot + 1) * 512]
                            ins = v.scalar_tensor_tensor(
                                dst, pb[tc][:], wm[:, tc * 8 + ie: tc * 8 + ie + 1], dst,
                                op0=ALU.mult, op1=ALU.add)
                            s_ev.inc(ins)
                    nb += 1

            @blk.sync
            def _(sy):
                sy.wait_ge(s_ev.h, s_ev.n)
                tgt = y_out if do_output else xspill
                ins = sy.dma_start(
                    out=tgt[rows:rows + HALF, :].rearrange("(c p) d -> p c d", p=128),
                    in_=xr[:].rearrange("p (c d) -> p c d", c=NTH))
                (s_od if do_output else s_sp).inc(ins, 16)

            barrier()

    a1x = at("a1x", [128, NTH * DM], F32, R1)

    def emit_a1():
        ve_group_done = [0]
        for grp in range(2):
            rows = grp * HALF

            @blk.sync
            def _(sy):
                if grp == 0:
                    sy.wait_ge(s_sp.h, s_sp.n)
                else:
                    sy.wait_ge(s_ve.h, ve_group_done[0])
                ins = sy.dma_start(
                    out=a1x[:].rearrange("p (c d) -> p c d", c=NTH),
                    in_=xspill[rows:rows + HALF, :].rearrange("(c p) d -> p c d", p=128))
                s_x.inc(ins, 16)

            xl = s_x.n

            @blk.vector
            def _(v):
                v.wait_ge(s_x.h, xl)

            emit_ln_batch([a1x[:, tc * DM:(tc + 1) * DM] for tc in range(NTH)],
                          lnT, lambda ci, g=grp: (g * NTH + ci) * 128,
                          rms_fold=True, accum_rows=rows)
            ve_group_done[0] = s_ve.n

        barrier()

    def emit_a2():
        ev_base = s_ev.n
        for jb in range(8):
            @blk.gpsimd
            def _(g):
                w_stream(g, "B", wqk1,
                         wqkT_p1[:, jb * 512:(jb + 1) * 512]
                         .rearrange("(dc p) j -> p dc j", p=128))

            ld_s = wsets["B"]["ld"][(wsets["B"]["i"] - 1) % 2]
            ld_t = ld_s.n

            @blk.tensor
            def _(t):
                t.wait_ge(ld_s.h, ld_t)
                for tc in range(NT):
                    if tc >= 8:
                        t.wait_ge(s_ev.h, ev_base + jb * 16 + (tc - 8) + 1)
                    elif jb > 0:
                        t.wait_ge(s_ev.h, ev_base + (jb - 1) * 16 + (tc + 8) + 1)
                    for dt in range(ND):
                        ins = t.matmul(pb[tc % 8][:],
                                       lnT[:, dt * S + tc * 128: dt * S + tc * 128 + 128],
                                       wqk1[jb % 2][:, dt * 512:(dt + 1) * 512],
                                       start=(dt == 0), stop=(dt == ND - 1))
                    s_pe.inc(ins)
            wsets["B"]["done"].append((s_pe, s_pe.n))

            pe_base = s_pe.n - NT

            @blk.vector
            def _(v):
                for tc in range(NT):
                    v.wait_ge(s_pe.h, pe_base + tc + 1)
                    col = (tc * 8 + jb) * 6
                    ins = v.bn_stats(stats_all[:, col:col + 6], pb[tc % 8][:])
                    s_ev.inc(ins)
                v.wait_ge(s_ev.h, s_ev.n)

        for tc in range(NT):
            @blk.vector
            def _(v):
                v.wait_ge(s_se.h, s_se.n)
                base = tc * 48
                L = lnsc
                vedge(v, v.bn_aggr(L[:, 24:26], stats_all[:, base:base + 24]))
                vedge(v, v.bn_aggr(L[:, 26:28], stats_all[:, base + 24:base + 48]))
                vedge(v, v.tensor_scalar(L[:, 29:30], L[:, 24:25], L[:, 24:25], None,
                                         op0=ALU.mult))
                vedge(v, v.tensor_tensor(L[:, 29:30], L[:, 29:30], L[:, 25:26], op=ALU.add))
                vedge(v, v.tensor_scalar(L[:, 31:32], L[:, 26:27], L[:, 26:27], None,
                                         op0=ALU.mult))
                vedge(v, v.tensor_tensor(L[:, 31:32], L[:, 31:32], L[:, 27:28], op=ALU.add))

            @blk.scalar
            def _(e):
                e.wait_ge(s_ve.h, s_ve.n)
                sedge(e, e.activation(rq_all[:, tc:tc + 1], lnsc[:, 29:30], AF.Ln,
                                      bias=ceps[:, 1:2]))
                sedge(e, e.activation(rq_all[:, tc:tc + 1], rq_all[:, tc:tc + 1], AF.Exp,
                                      scale=-0.5))
                sedge(e, e.activation(rk_all[:, tc:tc + 1], lnsc[:, 31:32], AF.Ln,
                                      bias=ceps[:, 1:2]))
                sedge(e, e.activation(rk_all[:, tc:tc + 1], rk_all[:, tc:tc + 1], AF.Exp,
                                      scale=-0.5))

        for which, src in ((0, rq_all), (1, rk_all)):
            @blk.tensor
            def _(t):
                t.wait_ge(s_se.h, s_se.n)
                t.wait_ge(s_ev.h, s_ev.n)
                ins = t.transpose(pb[0][0:NT, 0:128], src[:], ident_f[:])
                s_pe.inc(ins)

            @blk.vector
            def _(v):
                v.wait_ge(s_pe.h, s_pe.n)
                v.wait_ge(s_rr.h, s_rr.n)
                ins = v.tensor_copy(rqkT[:], pb[0][0:NT, 0:128])
                s_ev.inc(ins)
                v.wait_ge(s_ev.h, s_ev.n)

            @blk.sync
            def _(sy):
                sy.wait_ge(s_ev.h, s_ev.n)
                ins = sy.dma_start(out=scr_r[which], in_=rqkT[:])
                s_rr.inc(ins, 16)

        barrier()

    def emit_a3():
        carry_kv = stats_all[:, 0:NT]
        carry_w = stats_all[:, NT:2 * NT]

        @blk.vector
        def _(v):
            for sp_ in adp_sems:
                v.wait_ge(sp_.h, sp_.n)
            v.memset(carry_kv, 0.0)
            vedge(v, v.memset(carry_w, 0.0))

        @blk.gpsimd
        def _(g):
            for sp_ in adp_sems:
                g.wait_ge(sp_.h, sp_.n)


        s_attf = S_("attf")

        for half in range(2):
            @blk.sync
            def _(sy):
                sy.wait_ge(s_rr.h, s_rr.n)
                sy.wait_ge(s_ad.h, s_ad.n)
                s_x.inc(sy.dma_start(out=rqh_row[:],
                        in_=scr_r[0].rearrange("c p -> (c p)").unsqueeze(0)
                        [:, half * HALF:(half + 1) * HALF]), 16)
                ins = sy.dma_start(out=rkh_row[:],
                                   in_=scr_r[1].rearrange("c p -> (c p)").unsqueeze(0)
                                   [:, half * HALF:(half + 1) * HALF])
                s_x.inc(ins, 16)

            xl = s_x.n

            @blk.tensor
            def _(t):
                t.wait_ge(s_x.h, xl)
                t.wait_ge(s_attf.h, s_attf.n)
                for i in range(2):
                    t.matmul(pb[i][:], ones1_f[:], rqh_row[:, i * 512:(i + 1) * 512],
                             start=True, stop=True)
                for i in range(2):
                    ins = t.matmul(pb[2 + i][:], ones1_f[:], rkh_row[:, i * 512:(i + 1) * 512],
                                   start=True, stop=True)
                s_pe.inc(ins)

            @blk.vector
            def _(v):
                v.wait_ge(s_pe.h, s_pe.n)
                v.tensor_copy(rqb[:, 0:512], pb[0][:])
                v.tensor_copy(rqb[:, 512:1024], pb[1][:])
                v.tensor_copy(rkb[:, 0:512], pb[2][:])
                ins = v.tensor_copy(rkb[:, 512:1024], pb[3][:])
                s_attf.inc(ins)
                v.wait_ge(s_attf.h, s_attf.n)

            for fc in range(ND):
                pe_marks = []
                for gi, jc in ((0, fc), (1, ND + fc), (2, 2 * ND + fc)):
                    @blk.gpsimd
                    def _(g):
                        w_stream(g, "C", wst,
                                 wqkvT_p2[jc].rearrange("(dc p) j -> p dc j", p=128))

                    ld_s = wsets["C"]["ld"][(wsets["C"]["i"] - 1) % 2]
                    ld_t = ld_s.n
                    banks = ((0, 1), (2, 3), (4, 5))[gi]

                    @blk.tensor
                    def _(t):
                        t.wait_ge(ld_s.h, ld_t)
                        if gi == 0:
                            t.wait_ge(s_attf.h, s_attf.n)
                        for dt in range(ND):
                            for ni in range(2):
                                ins = t.matmul(
                                    pb[banks[ni]][:],
                                    wst[(wsets["C"]["i"] - 1) % 2][:, dt * 128:(dt + 1) * 128],
                                    lnT[:, dt * S + half * HALF + ni * 512:
                                        dt * S + half * HALF + (ni + 1) * 512],
                                    start=(dt == 0), stop=(dt == ND - 1))
                        s_pe.inc(ins)
                    wsets["C"]["done"].append((s_pe, s_pe.n))
                    pe_marks.append(s_pe.n)

                pe_q, pe_k, pe_v = pe_marks

                @blk.vector
                def _(v):
                    v.wait_ge(s_pe.h, pe_k)
                    v.tensor_tensor(at1[:, 0:512], pb[2][:], rkb[:, 0:512], op=ALU.mult)
                    vedge(v, v.tensor_tensor(at1[:, 512:1024], pb[3][:], rkb[:, 512:1024],
                                             op=ALU.mult))

                @blk.scalar
                def _(e):
                    e.wait_ge(s_ve.h, s_ve.n)
                    sedge(e, e.activation(at1[:], at1[:], AF.Exp))

                @blk.vector
                def _(v):
                    v.wait_ge(s_se.h, s_se.n)
                    v.wait_ge(s_pe.h, pe_v)
                    v.tensor_tensor(at2[:, 0:512], pb[4][:], at1[:, 0:512], op=ALU.mult)
                    vedge(v, v.tensor_tensor(at2[:, 512:1024], pb[5][:], at1[:, 512:1024],
                                             op=ALU.mult))
                    v.wait_ge(s_pe.h, pe_q)
                    v.tensor_tensor(at3[:, 0:512], pb[0][:], rqb[:, 0:512], op=ALU.mult)
                    ins = v.tensor_tensor(at3[:, 512:1024], pb[1][:], rqb[:, 512:1024],
                                          op=ALU.mult)
                    s_attf.inc(ins)
                    v.wait_ge(s_attf.h, s_attf.n)
                    vedge(v, v.tensor_tensor_scan(at4[:], at1[:], zeros[:],
                                                  carry_w[:, fc:fc + 1],
                                                  op0=ALU.add, op1=ALU.add))
                    vedge(v, v.tensor_copy(carry_w[:, fc:fc + 1], at4[:, HALF - 1:HALF]))
                    vedge(v, v.tensor_tensor_scan(at1[:], at2[:], zeros[:],
                                                  carry_kv[:, fc:fc + 1],
                                                  op0=ALU.add, op1=ALU.add))
                    vedge(v, v.tensor_copy(carry_kv[:, fc:fc + 1], at1[:, HALF - 1:HALF]))

                @blk.scalar
                def _(e):
                    e.wait_ge(s_ve.h, s_ve.n)
                    sedge(e, e.activation(at3[:], at3[:], AF.Exp, scale=-1.0))
                    sedge(e, e.activation(at3[:], at3[:], AF.Ln, bias=1.0))
                    sedge(e, e.activation(at3[:], at3[:], AF.Exp, scale=-1.0))
                    sedge(e, e.activation(at4[:], at4[:], AF.Ln, bias=ceps[:, 2:3]))
                    sedge(e, e.activation(at4[:], at4[:], AF.Exp, scale=-1.0))

                @blk.vector
                def _(v):
                    v.wait_ge(s_se.h, s_se.n)
                    vedge(v, v.tensor_tensor(at1[:], at1[:], at4[:], op=ALU.mult))
                    vedge(v, v.tensor_tensor(
                        yT[:, fc * S + half * HALF: fc * S + (half + 1) * HALF],
                        at1[:], at3[:], op=ALU.mult))

            barrier()

    def emit_p5():
        s_swh = S_("swh")
        s_opv = S_("opv")
        s_adp = adp_sems

        for half in range(2):
            for sg in range(4):
                swh_base = s_swh.n
                opv_sg_guard = s_opv.n
                for mi in range(16):
                    m = sg * 16 + mi
                    pe_marks = []
                    for gi, src_blk in ((0, wuT_blk), (1, wgT_blk)):
                        @blk.gpsimd
                        def _(g):
                            w_stream(g, "C", wst,
                                     src_blk[m].rearrange("(dc p) j -> p dc j", p=128))

                        ld_s = wsets["C"]["ld"][(wsets["C"]["i"] - 1) % 2]
                        ld_t = ld_s.n
                        banks = (((0, 1), (2, 3)) if mi % 2 == 0 else ((4, 5), (6, 7)))[gi]

                        @blk.tensor
                        def _(t):
                            t.wait_ge(ld_s.h, ld_t)
                            if gi == 0:
                                if mi >= 2:
                                    t.wait_ge(s_swh.h, swh_base + mi - 1)
                                if mi == 0:
                                    t.wait_ge(s_opv.h, opv_sg_guard)
                            for dt in range(ND):
                                for ni in range(2):
                                    ins = t.matmul(
                                        pb[banks[ni]][:],
                                        wst[(wsets["C"]["i"] - 1) % 2][:, dt * 128:(dt + 1) * 128],
                                        yT[:, dt * S + half * HALF + ni * 512:
                                           dt * S + half * HALF + (ni + 1) * 512],
                                        start=(dt == 0), stop=(dt == ND - 1))
                            s_pe.inc(ins)
                        wsets["C"]["done"].append((s_pe, s_pe.n))
                        pe_marks.append(s_pe.n)

                    pe_u, pe_g = pe_marks
                    bu, bg = (((0, 1), (2, 3)) if mi % 2 == 0 else ((4, 5), (6, 7)))

                    @blk.scalar
                    def _(e):
                        e.wait_ge(s_pe.h, pe_g)
                        e.wait_ge(s_swh.h, swh_base + mi)
                        e.activation(silu_t[:, 0:512], pb[bg[0]][:], AF.Silu)
                        sedge(e, e.activation(silu_t[:, 512:1024], pb[bg[1]][:], AF.Silu))

                    @blk.vector
                    def _(v):
                        v.wait_ge(s_se.h, s_se.n)
                        v.wait_ge(s_pe.h, pe_u)
                        col = mi * HALF
                        v.tensor_tensor(h_sg[:, col:col + 512], pb[bu[0]][:],
                                        silu_t[:, 0:512], op=ALU.mult)
                        ins = v.tensor_tensor(h_sg[:, col + 512:col + 1024], pb[bu[1]][:],
                                              silu_t[:, 512:1024], op=ALU.mult)
                        s_swh.inc(ins)
                        v.wait_ge(s_swh.h, s_swh.n)

                for oq in range(4):
                    @blk.gpsimd
                    def _(g):
                        w_stream(g, "D", wob,
                                 woutT[sg * DM:(sg + 1) * DM, oq * 512:(oq + 1) * 512]
                                 .rearrange("(dc p) o -> p dc o", p=128))

                    ld_s = wsets["D"]["ld"][(wsets["D"]["i"] - 1) % 2]
                    ld_t = ld_s.n
                    opv_base = s_opv.n
                    ad_base = [s_adp[0].n, s_adp[1].n]

                    @blk.tensor
                    def _(t):
                        t.wait_ge(ld_s.h, ld_t)
                        t.wait_ge(s_swh.h, swh_base + 16)
                        for tc in range(NTH):
                            if tc >= 2:
                                t.wait_ge(s_opv.h, opv_base + tc - 1)
                            elif oq == 0 and sg == 0:
                                pass
                            else:
                                t.wait_ge(s_opv.h, opv_base - 2 + tc + 1)
                            for dffc in range(16):
                                ins = t.matmul(
                                    pb[tc % 2][:],
                                    h_sg[:, dffc * HALF + tc * 128: dffc * HALF + tc * 128 + 128],
                                    wob[(wsets["D"]["i"] - 1) % 2][:, dffc * 512:(dffc + 1) * 512],
                                    start=(dffc == 0), stop=(dffc == 15))
                            s_pe.inc(ins)
                    wsets["D"]["done"].append((s_pe, s_pe.n))

                    pe_base = s_pe.n - NTH

                    @blk.vector
                    def _(v):
                        for tc in range(NTH):
                            v.wait_ge(s_pe.h, pe_base + tc + 1)
                            v.wait_ge(s_adp[tc % 2].h, ad_base[tc % 2] + (tc // 2) * 16)
                            ins = v.tensor_copy(tmpout[tc % 2][:], pb[tc % 2][:])
                            s_opv.inc(ins)

                    @blk.gpsimd
                    def _(g):
                        for tc in range(NTH):
                            g.wait_ge(s_opv.h, opv_base + tc + 1)
                            rows = half * HALF + tc * 128
                            ins = g.dma_start(
                                out=xspill[rows:rows + 128, oq * 512:(oq + 1) * 512],
                                in_=tmpout[tc % 2][:], accum_op=ALU.add)
                            s_adp[tc % 2].inc(ins, 16)

            barrier()

    if "ffn1" in stages:
        emit_ffn(0, xr1, x_in, do_output=(stages == ("ffn1",)))
    if "attn" in stages:
        emit_a1()
        emit_a2()
        emit_a3()
    if "proj" in stages:
        emit_p5()
    if "ffn2" in stages:
        emit_ffn(1, xr2, xspill, do_output=True)
    elif stages != ("ffn1",):
        @blk.sync
        def _(sy):
            for tc in range(NT):
                ins = sy.dma_start(out=xrt[0][:], in_=xspill[tc * 128:(tc + 1) * 128, :])
                s_x.inc(ins, 16)
                sy.wait_ge(s_x.h, s_x.n)
                ins = sy.dma_start(out=y_out[tc * 128:(tc + 1) * 128, :], in_=xrt[0][:])
                s_od.inc(ins, 16)
                sy.wait_ge(s_od.h, s_od.n)

    @blk.sync
    def _(sy):
        sy.wait_ge(s_od.h, s_od.n)

    blk_ctx.__exit__(None, None, None)
    ctx.close()
    return nc


def np_split_inputs(inputs):
    x = np.asarray(inputs["L0_attn_x"], dtype=np.float32)
    w_qkv = np.asarray(inputs["L0_attn_w_qkv"], dtype=np.float32)
    w_sw = np.asarray(inputs["L0_attn_w_swiglu"], dtype=np.float32)
    w_out = np.asarray(inputs["L0_attn_w_out"], dtype=np.float32)
    gate_w = np.asarray(inputs["L0_ffn_gate_w"], dtype=np.float32)
    exp_w = np.asarray(inputs["L0_ffn_expert_w"], dtype=np.float32)
    exp_b = np.asarray(inputs["L0_ffn_expert_b"], dtype=np.float32)

    wqkvT = np.ascontiguousarray(w_qkv.T)
    shared = {
        "wqkvT_p2": np.ascontiguousarray(wqkvT.reshape(DM, 48, 128).transpose(1, 0, 2)),
        "wqkT_p1": np.ascontiguousarray(wqkvT[:, :2 * DM]),
        "wuT_blk": np.ascontiguousarray(
            w_sw[:DFF].T.reshape(DM, 64, 128).transpose(1, 0, 2)),
        "wgT_blk": np.ascontiguousarray(
            w_sw[DFF:].T.reshape(DM, 64, 128).transpose(1, 0, 2)),
        "woutT": np.ascontiguousarray(w_out.T),
        "gateT": np.ascontiguousarray(gate_w.T),
        "expb": np.ascontiguousarray(exp_b),
        "expT_blk": np.ascontiguousarray(
            exp_w.transpose(0, 2, 1).reshape(NE, DM, 4, 512).transpose(0, 2, 1, 3)),
    }
    return [dict(shared, x=np.ascontiguousarray(x[c])) for c in range(N_CORES)]


def kernel(**inputs):
    in_maps = np_split_inputs(inputs)
    nc = build_program()
    res = run_bass_kernel_spmd(nc, in_maps, core_ids=list(range(N_CORES)))
    return np.stack([np.asarray(r["y"], dtype=np.float32) for r in res.results], axis=0)


# version 03
# speedup vs baseline: 1.0882x; 1.0882x over previous
import numpy as np
import ml_dtypes
import concourse.bass as bass
import concourse.mybir as mybir
from concourse.bass_utils import run_bass_kernel_spmd
from contextlib import ExitStack

F32 = mybir.dt.float32
BF16 = mybir.dt.bfloat16
AF = mybir.ActivationFunctionType
ALU = mybir.AluOpType
AX = mybir.AxisListType

S = 2048
DM = 2048
DFF = 8192
NE = 8
NT = S // 128
ND = DM // 128
HALF = S // 2
NTH = NT // 2
EPS_RMS = 1.1920929e-07
EPS_LN = 1e-5
N_CORES = 8
SB_BASE = 20448
SB_LIMIT = 229344


class Sem:
    __slots__ = ("h", "n")

    def __init__(self, nc, name):
        self.h = nc.alloc_semaphore(name)
        self.n = 0

    def inc(self, instr, k=1):
        instr.then_inc(self.h, k)
        self.n += k
        return instr


def build_program(stages=("ffn1", "attn", "proj", "ffn2")):
    nc = bass.Bass("TRN2", target_bir_lowering=False)
    d = nc.dram_tensor
    x_in = d("x", [S, DM], F32, kind="ExternalInput").ap()
    wqkvT_p2 = d("wqkvT_p2", [48, DM, 128], F32, kind="ExternalInput").ap()
    wqkT_p1 = d("wqkT_p1", [DM, 2 * DM], F32, kind="ExternalInput").ap()
    wuT_blk = d("wuT_blk", [64, DM, 128], F32, kind="ExternalInput").ap()
    wgT_blk = d("wgT_blk", [64, DM, 128], F32, kind="ExternalInput").ap()
    woutT = d("woutT", [DFF, DM], F32, kind="ExternalInput").ap()
    gateT = d("gateT", [DM, NE], F32, kind="ExternalInput").ap()
    expb = d("expb", [NE, DM], F32, kind="ExternalInput").ap()
    expT_blk = d("expT_blk", [NE, 4, DM, 512], F32, kind="ExternalInput").ap()
    y_out = d("y", [S, DM], F32, kind="ExternalOutput").ap()
    xspill = d("xspill", [S, DM], F32, kind="Internal").ap()
    scr_r = d("scr_r", [2, NT, 128], F32, kind="Internal").ap()

    c_ident_bf = nc.inline_tensor(np.eye(128, dtype=ml_dtypes.bfloat16), "c_ib").ap()
    c_ident_f = nc.inline_tensor(np.eye(128, dtype=np.float32), "c_if").ap()
    c_eps = nc.inline_tensor(
        np.tile(np.array([[EPS_LN, EPS_RMS, 1e-6, 0.0]], dtype=np.float32), (128, 1)),
        "c_eps").ap()

    def at(name, shape, dtype, offset):
        offset += SB_BASE
        per = int(np.prod(shape[1:])) * mybir.dt.size(dtype)
        assert offset + per <= SB_LIMIT, f"sbuf overflow {name}: {offset + per}"
        return nc.alloc_sbuf_tensor_at(name, list(shape), dtype, offset=offset)

    o = [0]

    def nxt(name, shape, dtype):
        per = int(np.prod(shape[1:])) * mybir.dt.size(dtype)
        off = (o[0] + 31) // 32 * 32
        t = at(name, shape, dtype, off)
        o[0] = off + per
        return t

    ident_bf = nxt("ident_bf", [128, 128], BF16)
    ident_f = nxt("ident_f", [128, 128], F32)
    ones1_f = nxt("ones1_f", [1, 128], F32)
    gateT_bf = nxt("gateT_bf", [128, ND * NE], BF16)
    expb_bf = nxt("expb_bf", [NE, DM], BF16)
    wmaskT = nxt("wmaskT", [NE, S], F32)
    wmaskT_bf = nxt("wmaskT_bf", [NE, S], BF16)
    wm = nxt("wm", [128, NTH * NE], F32)
    gsc = nxt("gsc", [128, 32], F32)
    lnsc = nxt("lnsc", [128, 48], F32)
    rq_all = nxt("rq_all", [128, NT], F32)
    rk_all = nxt("rk_all", [128, NT], F32)
    rqkT = nxt("rqkT", [NT, 128], F32)
    ln_tm_off = (o[0] + 31) // 32 * 32
    ln_tm = nxt("ln_tm", [128, DM], BF16)
    ln_f32 = nxt("ln_f32", [128, DM], F32)
    rqh_row = at("rqh_row", [1, HALF], F32, ln_tm_off)
    rkh_row = at("rkh_row", [1, HALF], F32, ln_tm_off + 4096)
    zeros = nxt("zeros", [128, HALF], F32)
    stats_all = nxt("stats_all", [128, NT * 8 * 6], F32)
    rkb = nxt("rkb", [128, HALF], F32)
    ceps = nxt("ceps", [128, 4], F32)
    assert o[0] <= 49152, o[0]

    R1, R2, R2B, R3 = 49152, 114688, 147456, 180224
    xr1 = at("xr1", [128, NTH * DM], F32, R1)
    xrt = [at(f"xrt{i}", [128, DM], F32, R1 + i * 8192) for i in range(2)]
    wqk1 = [at(f"wqk1_{i}", [128, 16 * 512], BF16, R1 + i * 16384) for i in range(2)]
    yT = at("yT", [128, ND * S], BF16, R1)
    xr2 = at("xr2", [128, NTH * DM], F32, R1)
    lnT = at("lnT", [128, ND * S], BF16, R2)
    lnTh = at("lnTh", [128, ND * HALF], BF16, R2)
    wblk = [at(f"wblk{i}", [128, 16 * 512], BF16, R2B + i * 16384) for i in range(2)]
    h_sg = at("h_sg", [128, 16 * HALF], BF16, R2)
    wob = [at(f"wob{i}", [128, 16 * 512], BF16, R2B + i * 16384) for i in range(2)]
    wst = [at(f"wst{i}", [128, ND * 128], BF16, R3 + i * 4096) for i in range(2)]
    at1 = at("at1", [128, HALF], F32, R3 + 8192)
    at2 = at("at2", [128, HALF], F32, R3 + 12288)
    at3 = at("at3", [128, HALF], F32, R3 + 16384)
    at4 = at("at4", [128, HALF], F32, R3 + 20480)
    rqb = at("rqb", [128, HALF], F32, R3 + 24576)
    silu_t = at("silu_t", [128, HALF], F32, R3 + 8192)
    tmpout = [at(f"tmpout{i}", [128, 512], F32, R3 + 12288 + i * 2048) for i in range(2)]

    ctx = ExitStack()
    pb = [ctx.enter_context(nc.psum_tensor(f"pb{i}", [128, 512], F32)) for i in range(8)]

    def pbf(i):
        return pb[i][:].bitcast(BF16)

    sems = {}

    def S_(name):
        if name not in sems:
            sems[name] = Sem(nc, "s_" + name)
        return sems[name]

    s_x = S_("x")
    s_pe = S_("pe")
    s_ve = S_("ve")
    s_se = S_("se")
    s_ev = S_("ev")
    s_sp = S_("sp")
    s_ad = S_("ad")
    s_od = S_("od")
    s_rr = S_("rr")
    adp_sems = [S_("adp0"), S_("adp1")]
    wsets = {}
    for nm in ("A", "B", "C", "D"):
        wsets[nm] = {"ld": [S_("wl" + nm + "0"), S_("wl" + nm + "1")], "i": 0, "done": []}

    def w_stream(g, setname, buf_list, src_ap):
        ws = wsets[setname]
        i = ws["i"]
        buf = buf_list[i % 2]
        if i >= 2:
            sem, val = ws["done"][i - 2]
            g.wait_ge(sem.h, val)
        nblk = src_ap.shape[1]
        ws["ld"][i % 2].inc(g.dma_start(
            out=buf[:].rearrange("p (b i) -> p b i", b=nblk), in_=src_ap), 16)
        ws["i"] = i + 1
        return buf

    def vedge(v, instr):
        s_ve.inc(instr)
        v.wait_ge(s_ve.h, s_ve.n)

    def sedge(e, instr):
        s_se.inc(instr)
        e.wait_ge(s_se.h, s_se.n)

    blk_ctx = nc.Block()
    blk = blk_ctx.__enter__()

    s_bar = S_("bar")
    bar_n = [0]

    def barrier():
        bar_n[0] += 1
        tgt = 5 * bar_n[0]
        for ename in ("sync", "tensor", "vector", "scalar", "gpsimd"):
            def _f(e, tgt=tgt):
                e.drain()
                e.nop().then_inc(s_bar.h, 1)
                e.wait_ge(s_bar.h, tgt)
            getattr(blk, ename)(_f)

    s_cg = S_("cg")

    @blk.sync
    def _(sy):
        s_x.inc(sy.dma_start(out=ident_bf[:], in_=c_ident_bf[:]), 16)
        s_x.inc(sy.dma_start(out=ceps[:], in_=c_eps[:]), 16)
        s_x.inc(sy.dma_start(out=ident_f[:], in_=c_ident_f[:]), 16)
        sy.wait_ge(s_x.h, s_x.n)

    @blk.gpsimd
    def _(g):
        s_cg.inc(g.dma_start(out=gateT_bf[:].rearrange("p (c e) -> p c e", e=NE),
                             in_=gateT.rearrange("(c p) e -> p c e", p=128)), 16)
        s_cg.inc(g.dma_start(out=expb_bf[:], in_=expb), 16)
        g.wait_ge(s_cg.h, s_cg.n)

    @blk.vector
    def _(v):
        v.memset(ones1_f[:], 1.0)
        s_ve.inc(v.memset(zeros[:], 0.0))

    barrier()

    s_lnt = S_("lnt")
    ln2_tm = [at(f"ln2tm{i}", [128, DM], BF16, R3 + i * 4096) for i in range(2)]
    a1f32 = [at(f"a1f{i}", [128, DM], F32, R3 + 8192 + i * 8192) for i in range(2)]

    def emit_ln_batch(srcs, dst_lnT, col_of, rms_fold, accum_rows=None):
        n = len(srcs)
        L = lnsc
        mean_str = L[:, 0:2 * n:2]
        var_str = L[:, 1:2 * n:2]

        @blk.vector
        def _(v):
            for ci in range(n):
                for i in range(4):
                    ins = v.bn_stats(stats_all[:, ci * 24 + i * 6: ci * 24 + (i + 1) * 6],
                                     srcs[ci][:, i * 512:(i + 1) * 512])
            vedge(v, ins)
            for ci in range(n):
                ins = v.bn_aggr(L[:, 2 * ci:2 * ci + 2], stats_all[:, ci * 24:ci * 24 + 24])
            vedge(v, ins)

        @blk.scalar
        def _(e):
            e.wait_ge(s_ve.h, s_ve.n)
            sedge(e, e.activation(L[:, 16:16 + n], var_str, AF.Ln, bias=ceps[:, 0:1]))
            sedge(e, e.activation(L[:, 16:16 + n], L[:, 16:16 + n], AF.Exp, scale=-0.5))

        if rms_fold:
            @blk.vector
            def _(v):
                v.wait_ge(s_se.h, s_se.n)
                vedge(v, v.tensor_tensor(L[:, 40:40 + n], L[:, 16:16 + n],
                                         L[:, 16:16 + n], op=ALU.mult))
                vedge(v, v.tensor_tensor(L[:, 40:40 + n], var_str, L[:, 40:40 + n],
                                         op=ALU.mult))

            @blk.scalar
            def _(e):
                e.wait_ge(s_ve.h, s_ve.n)
                sedge(e, e.activation(L[:, 40:40 + n], L[:, 40:40 + n], AF.Ln,
                                      bias=ceps[:, 1:2]))
                sedge(e, e.activation(L[:, 40:40 + n], L[:, 40:40 + n], AF.Exp, scale=-0.5))

        @blk.vector
        def _(v):
            v.wait_ge(s_se.h, s_se.n)
            if rms_fold:
                vedge(v, v.tensor_tensor(L[:, 24:24 + n], L[:, 16:16 + n], L[:, 40:40 + n],
                                         op=ALU.mult))
            else:
                vedge(v, v.tensor_copy(L[:, 24:24 + n], L[:, 16:16 + n]))
            ins = v.scalar_tensor_tensor(L[:, 32:32 + n], mean_str, -1.0, L[:, 24:24 + n],
                                         op0=ALU.mult, op1=ALU.mult)
            vedge(v, ins)

        affine_ve = [0] * n
        ad_marks = [adp_sems[0].n, adp_sems[1].n]
        width = dst_lnT.shape[1] // ND

        def emit_affine(ci):
            @blk.vector
            def _(v):
                if ci >= 2:
                    v.wait_ge(s_pe.h, tr_done[ci - 2])
                ins = v.tensor_scalar(ln2_tm[ci % 2][:], srcs[ci][:],
                                      L[:, 24 + ci:25 + ci], L[:, 32 + ci:33 + ci],
                                      op0=ALU.mult, op1=ALU.add)
                if accum_rows is None:
                    vedge(v, ins)
                else:
                    s_ve.inc(ins)
                    v.wait_ge(adp_sems[ci % 2].h, ad_marks[ci % 2])
                    ins = v.tensor_scalar(a1f32[ci % 2][:], srcs[ci][:],
                                          L[:, 24 + ci:25 + ci], L[:, 32 + ci:33 + ci],
                                          op0=ALU.mult, op1=ALU.add)
                    vedge(v, ins)
            affine_ve[ci] = s_ve.n

        tr_done = [0] * n
        emit_affine(0)
        for ci in range(n):
            col = col_of(ci)

            @blk.tensor
            def _(t):
                t.wait_ge(s_ve.h, affine_ve[ci])
                t.wait_ge(s_lnt.h, s_lnt.n)
                for h8 in range(2):
                    bank = 6 + h8
                    for i in range(8):
                        dt = h8 * 8 + i
                        ins = t.transpose(pbf(bank)[:, i * 128:(i + 1) * 128],
                                          ln2_tm[ci % 2][:, dt * 128:(dt + 1) * 128],
                                          ident_bf[:])
                    s_pe.inc(ins)
            tr_done[ci] = s_pe.n

            if accum_rows is not None:
                @blk.gpsimd
                def _(g):
                    g.wait_ge(s_ve.h, affine_ve[ci])
                    rows = accum_rows + ci * 128
                    ins = g.dma_start(out=xspill[rows:rows + 128, :],
                                      in_=a1f32[ci % 2][:], accum_op=ALU.add)
                    adp_sems[ci % 2].inc(ins, 16)
                ad_marks[ci % 2] = adp_sems[ci % 2].n

            if ci + 1 < n:
                emit_affine(ci + 1)

            @blk.vector
            def _(v):
                for h8 in range(2):
                    v.wait_ge(s_pe.h, tr_done[ci] - 1 + h8)
                    bank = 6 + h8
                    src = pbf(bank)[:, 0:1024].rearrange("p (b i) -> p b i", b=8)
                    dsts = dst_lnT[:].rearrange("p (dt w) -> p dt w", dt=ND)[
                        :, h8 * 8:(h8 + 1) * 8, col:col + 128]
                    ins = v.tensor_copy(dsts, src)
                s_lnt.inc(ins)
                v.wait_ge(s_lnt.h, s_lnt.n)

    def emit_gating_batch(gbase, lnT_t):
        G = gsc
        glog = stats_all[:, 256:320]
        mk1 = stats_all[:, 320:384]
        mk2 = stats_all[:, 384:448]
        width = lnT_t.shape[1] // ND

        @blk.tensor
        def _(t):
            t.wait_ge(s_lnt.h, s_lnt.n)
            t.wait_ge(s_ev.h, s_ev.n)
            for ci in range(NTH):
                for dt in range(ND):
                    ins = t.matmul(
                        pb[5][:, ci * 8:(ci + 1) * 8],
                        lnT_t[:, dt * width + ci * 128: dt * width + ci * 128 + 128],
                        gateT_bf[:, dt * 8:(dt + 1) * 8],
                        start=(dt == 0), stop=(dt == ND - 1))
            s_pe.inc(ins)

        @blk.vector
        def _(v):
            v.wait_ge(s_pe.h, s_pe.n)
            ins = v.tensor_copy(glog, pb[5][:, 0:64])
            s_ev.inc(ins)
            v.wait_ge(s_ev.h, s_ev.n)
            for ci in range(NTH):
                gl = glog[:, ci * 8:(ci + 1) * 8]
                vedge(v, v.reduce_max(G[:, ci:ci + 1], gl, axis=AX.X))
                vedge(v, v.tensor_scalar(mk1[:, ci * 8:(ci + 1) * 8], gl,
                                         G[:, ci:ci + 1], None, op0=ALU.is_ge))
                vedge(v, v.scalar_tensor_tensor(mk2[:, ci * 8:(ci + 1) * 8],
                                                mk1[:, ci * 8:(ci + 1) * 8], -1e30, gl,
                                                op0=ALU.mult, op1=ALU.add))
                vedge(v, v.reduce_max(G[:, 8 + ci:9 + ci],
                                      mk2[:, ci * 8:(ci + 1) * 8], axis=AX.X))
                vedge(v, v.tensor_scalar(mk2[:, ci * 8:(ci + 1) * 8],
                                         mk2[:, ci * 8:(ci + 1) * 8],
                                         G[:, 8 + ci:9 + ci], None, op0=ALU.is_ge))
            vedge(v, v.tensor_tensor(G[:, 16:24], G[:, 0:8], G[:, 8:16], op=ALU.subtract))

        @blk.scalar
        def _(e):
            e.wait_ge(s_ve.h, s_ve.n)
            sedge(e, e.activation(G[:, 24:32], G[:, 16:24], AF.Exp, scale=-1.0))
            sedge(e, e.activation(G[:, 24:32], G[:, 24:32], AF.Ln, bias=1.0))
            sedge(e, e.activation(G[:, 24:32], G[:, 24:32], AF.Exp, scale=-1.0))

        @blk.vector
        def _(v):
            v.wait_ge(s_se.h, s_se.n)
            vedge(v, v.tensor_scalar_mul(G[:, 24:32], G[:, 24:32], 0.5))
            vedge(v, v.tensor_scalar(G[:, 8:16], G[:, 24:32], -1.0, 0.5,
                                     op0=ALU.mult, op1=ALU.add))
            for ci in range(NTH):
                vedge(v, v.tensor_scalar_mul(mk1[:, ci * 8:(ci + 1) * 8],
                                             mk1[:, ci * 8:(ci + 1) * 8],
                                             G[:, 24 + ci:25 + ci]))
                vedge(v, v.tensor_scalar_mul(mk2[:, ci * 8:(ci + 1) * 8],
                                             mk2[:, ci * 8:(ci + 1) * 8],
                                             G[:, 8 + ci:9 + ci]))
            vedge(v, v.tensor_tensor(wm[:], mk1[:, 0:64], mk2[:, 0:64], op=ALU.add))

        for rnd in range(2):
            @blk.tensor
            def _(t):
                t.wait_ge(s_ve.h, s_ve.n)
                t.wait_ge(s_ev.h, s_ev.n)
                for j in range(4):
                    ci = rnd * 4 + j
                    ins = t.transpose(pb[5][0:8, j * 128:(j + 1) * 128],
                                      wm[:, ci * 8:(ci + 1) * 8], ident_f[:])
                s_pe.inc(ins)

            @blk.vector
            def _(v):
                v.wait_ge(s_pe.h, s_pe.n)
                gc = (gbase + rnd * 4) * 128
                ins = v.tensor_copy(wmaskT[0:8, gc:gc + 512], pb[5][0:8, 0:512])
                s_ev.inc(ins)
                v.wait_ge(s_ev.h, s_ev.n)

        @blk.vector
        def _(v):
            gc = gbase * 128
            vedge(v, v.tensor_copy(wmaskT_bf[0:8, gc:gc + HALF], wmaskT[0:8, gc:gc + HALF]))

    def emit_ffn(ffn_idx, xr, src_dram, do_output):
        for half in range(2):
            gbase = half * NTH
            rows = half * HALF

            @blk.sync
            def _(sy):
                if ffn_idx == 1:
                    sy.wait_ge(s_ad.h, s_ad.n)
                    for sp_ in adp_sems:
                        sy.wait_ge(sp_.h, sp_.n)
                sy.wait_ge(s_od.h, s_od.n)
                sy.wait_ge(s_sp.h, s_sp.n)
                ins = sy.dma_start(
                    out=xr[:].rearrange("p (c d) -> p c d", c=NTH),
                    in_=src_dram[rows:rows + HALF, :].rearrange("(c p) d -> p c d", p=128))
                s_x.inc(ins, 16)

            xload_t = s_x.n

            @blk.vector
            def _(v):
                v.wait_ge(s_x.h, xload_t)

            emit_ln_batch([xr[:, tc * DM:(tc + 1) * DM] for tc in range(NTH)],
                          lnTh, lambda ci: ci * 128, rms_fold=False)
            emit_gating_batch(gbase, lnTh)

            for tc in range(NTH):
                gc = (gbase + tc) * 128

                @blk.tensor
                def _(t):
                    t.wait_ge(s_ve.h, s_ve.n)
                    t.wait_ge(s_ev.h, s_ev.n)
                    for ot in range(4):
                        ins = t.matmul(pb[ot][:], wmaskT_bf[0:8, gc:gc + 128],
                                       expb_bf[0:8, ot * 512:(ot + 1) * 512],
                                       start=True, stop=True)
                        s_pe.inc(ins)

                @blk.vector
                def _(v):
                    v.wait_ge(s_pe.h, s_pe.n)
                    for ot in range(4):
                        dst = xr[:, tc * DM + ot * 512: tc * DM + (ot + 1) * 512]
                        ins = v.tensor_tensor(dst, dst, pb[ot][:], op=ALU.add)
                        s_ev.inc(ins)
                    v.wait_ge(s_ev.h, s_ev.n)

            ev_base = s_ev.n
            nb = 0
            for ie in range(NE):
                for ot in range(4):
                    @blk.gpsimd
                    def _(g):
                        w_stream(g, "A", wblk,
                                 expT_blk[ie, ot].rearrange("(dc p) o -> p dc o", p=128))

                    ld_s = wsets["A"]["ld"][(wsets["A"]["i"] - 1) % 2]
                    ld_t = ld_s.n
                    bi = nb

                    @blk.tensor
                    def _(t):
                        t.wait_ge(ld_s.h, ld_t)
                        if bi == 0:
                            t.wait_ge(s_ev.h, ev_base)
                        for tc in range(NTH):
                            if bi > 0:
                                t.wait_ge(s_ev.h, ev_base + (bi - 1) * NTH + tc + 1)
                            for dt in range(ND):
                                ins = t.matmul(
                                    pb[tc][:],
                                    lnTh[:, dt * HALF + tc * 128: dt * HALF + tc * 128 + 128],
                                    wblk[bi % 2][:, dt * 512:(dt + 1) * 512],
                                    start=(dt == 0), stop=(dt == ND - 1))
                            s_pe.inc(ins)
                    wsets["A"]["done"].append((s_pe, s_pe.n))

                    pe_base = s_pe.n - NTH

                    @blk.vector
                    def _(v):
                        for tc in range(NTH):
                            v.wait_ge(s_pe.h, pe_base + tc + 1)
                            dst = xr[:, tc * DM + ot * 512: tc * DM + (ot + 1) * 512]
                            ins = v.scalar_tensor_tensor(
                                dst, pb[tc][:], wm[:, tc * 8 + ie: tc * 8 + ie + 1], dst,
                                op0=ALU.mult, op1=ALU.add)
                            s_ev.inc(ins)
                    nb += 1

            @blk.sync
            def _(sy):
                sy.wait_ge(s_ev.h, s_ev.n)
                tgt = y_out if do_output else xspill
                ins = sy.dma_start(
                    out=tgt[rows:rows + HALF, :].rearrange("(c p) d -> p c d", p=128),
                    in_=xr[:].rearrange("p (c d) -> p c d", c=NTH))
                (s_od if do_output else s_sp).inc(ins, 16)

            barrier()

    a1x = at("a1x", [128, NTH * DM], F32, R1)

    def emit_a1():
        ve_group_done = [0]
        for grp in range(2):
            rows = grp * HALF

            @blk.sync
            def _(sy):
                if grp == 0:
                    sy.wait_ge(s_sp.h, s_sp.n)
                else:
                    sy.wait_ge(s_ve.h, ve_group_done[0])
                ins = sy.dma_start(
                    out=a1x[:].rearrange("p (c d) -> p c d", c=NTH),
                    in_=xspill[rows:rows + HALF, :].rearrange("(c p) d -> p c d", p=128))
                s_x.inc(ins, 16)

            xl = s_x.n

            @blk.vector
            def _(v):
                v.wait_ge(s_x.h, xl)

            emit_ln_batch([a1x[:, tc * DM:(tc + 1) * DM] for tc in range(NTH)],
                          lnT, lambda ci, g=grp: (g * NTH + ci) * 128,
                          rms_fold=True, accum_rows=rows)
            ve_group_done[0] = s_ve.n

        barrier()

    def emit_a2():
        ev_base = s_ev.n
        for jb in range(8):
            @blk.gpsimd
            def _(g):
                w_stream(g, "B", wqk1,
                         wqkT_p1[:, jb * 512:(jb + 1) * 512]
                         .rearrange("(dc p) j -> p dc j", p=128))

            ld_s = wsets["B"]["ld"][(wsets["B"]["i"] - 1) % 2]
            ld_t = ld_s.n

            @blk.tensor
            def _(t):
                t.wait_ge(ld_s.h, ld_t)
                for tc in range(NT):
                    if tc >= 8:
                        t.wait_ge(s_ev.h, ev_base + jb * 16 + (tc - 8) + 1)
                    elif jb > 0:
                        t.wait_ge(s_ev.h, ev_base + (jb - 1) * 16 + (tc + 8) + 1)
                    for dt in range(ND):
                        ins = t.matmul(pb[tc % 8][:],
                                       lnT[:, dt * S + tc * 128: dt * S + tc * 128 + 128],
                                       wqk1[jb % 2][:, dt * 512:(dt + 1) * 512],
                                       start=(dt == 0), stop=(dt == ND - 1))
                    s_pe.inc(ins)
            wsets["B"]["done"].append((s_pe, s_pe.n))

            pe_base = s_pe.n - NT

            @blk.vector
            def _(v):
                for tc in range(NT):
                    v.wait_ge(s_pe.h, pe_base + tc + 1)
                    col = (tc * 8 + jb) * 6
                    ins = v.bn_stats(stats_all[:, col:col + 6], pb[tc % 8][:])
                    s_ev.inc(ins)
                v.wait_ge(s_ev.h, s_ev.n)

        for tc in range(NT):
            @blk.vector
            def _(v):
                v.wait_ge(s_se.h, s_se.n)
                base = tc * 48
                L = lnsc
                vedge(v, v.bn_aggr(L[:, 24:26], stats_all[:, base:base + 24]))
                vedge(v, v.bn_aggr(L[:, 26:28], stats_all[:, base + 24:base + 48]))
                vedge(v, v.tensor_scalar(L[:, 29:30], L[:, 24:25], L[:, 24:25], None,
                                         op0=ALU.mult))
                vedge(v, v.tensor_tensor(L[:, 29:30], L[:, 29:30], L[:, 25:26], op=ALU.add))
                vedge(v, v.tensor_scalar(L[:, 31:32], L[:, 26:27], L[:, 26:27], None,
                                         op0=ALU.mult))
                vedge(v, v.tensor_tensor(L[:, 31:32], L[:, 31:32], L[:, 27:28], op=ALU.add))

            @blk.scalar
            def _(e):
                e.wait_ge(s_ve.h, s_ve.n)
                sedge(e, e.activation(rq_all[:, tc:tc + 1], lnsc[:, 29:30], AF.Ln,
                                      bias=ceps[:, 1:2]))
                sedge(e, e.activation(rq_all[:, tc:tc + 1], rq_all[:, tc:tc + 1], AF.Exp,
                                      scale=-0.5))
                sedge(e, e.activation(rk_all[:, tc:tc + 1], lnsc[:, 31:32], AF.Ln,
                                      bias=ceps[:, 1:2]))
                sedge(e, e.activation(rk_all[:, tc:tc + 1], rk_all[:, tc:tc + 1], AF.Exp,
                                      scale=-0.5))

        for which, src in ((0, rq_all), (1, rk_all)):
            @blk.tensor
            def _(t):
                t.wait_ge(s_se.h, s_se.n)
                t.wait_ge(s_ev.h, s_ev.n)
                ins = t.transpose(pb[0][0:NT, 0:128], src[:], ident_f[:])
                s_pe.inc(ins)

            @blk.vector
            def _(v):
                v.wait_ge(s_pe.h, s_pe.n)
                v.wait_ge(s_rr.h, s_rr.n)
                ins = v.tensor_copy(rqkT[:], pb[0][0:NT, 0:128])
                s_ev.inc(ins)
                v.wait_ge(s_ev.h, s_ev.n)

            @blk.sync
            def _(sy):
                sy.wait_ge(s_ev.h, s_ev.n)
                ins = sy.dma_start(out=scr_r[which], in_=rqkT[:])
                s_rr.inc(ins, 16)

        barrier()

    def emit_a3():
        carry_kv = stats_all[:, 0:NT]
        carry_w = stats_all[:, NT:2 * NT]

        @blk.vector
        def _(v):
            for sp_ in adp_sems:
                v.wait_ge(sp_.h, sp_.n)
            v.memset(carry_kv, 0.0)
            vedge(v, v.memset(carry_w, 0.0))

        @blk.gpsimd
        def _(g):
            for sp_ in adp_sems:
                g.wait_ge(sp_.h, sp_.n)


        s_attf = S_("attf")
        attf_marks = [0, 0, 0]

        for half in range(2):
            @blk.sync
            def _(sy):
                sy.wait_ge(s_rr.h, s_rr.n)
                sy.wait_ge(s_ad.h, s_ad.n)
                s_x.inc(sy.dma_start(out=rqh_row[:],
                        in_=scr_r[0].rearrange("c p -> (c p)").unsqueeze(0)
                        [:, half * HALF:(half + 1) * HALF]), 16)
                ins = sy.dma_start(out=rkh_row[:],
                                   in_=scr_r[1].rearrange("c p -> (c p)").unsqueeze(0)
                                   [:, half * HALF:(half + 1) * HALF])
                s_x.inc(ins, 16)

            xl = s_x.n

            @blk.tensor
            def _(t):
                t.wait_ge(s_x.h, xl)
                t.wait_ge(s_attf.h, attf_marks[1])
                for i in range(2):
                    t.matmul(pb[i][:], ones1_f[:], rqh_row[:, i * 512:(i + 1) * 512],
                             start=True, stop=True)
                for i in range(2):
                    ins = t.matmul(pb[2 + i][:], ones1_f[:], rkh_row[:, i * 512:(i + 1) * 512],
                                   start=True, stop=True)
                s_pe.inc(ins)

            @blk.vector
            def _(v):
                v.wait_ge(s_pe.h, s_pe.n)
                v.tensor_copy(rqb[:, 0:512], pb[0][:])
                ins = v.tensor_copy(rqb[:, 512:1024], pb[1][:])
                s_attf.inc(ins)
                v.tensor_copy(rkb[:, 0:512], pb[2][:])
                ins = v.tensor_copy(rkb[:, 512:1024], pb[3][:])
                s_attf.inc(ins)
                v.wait_ge(s_attf.h, s_attf.n)
            attf_marks[0] = s_attf.n - 1
            attf_marks[1] = s_attf.n

            for fc in range(ND):
                pe_marks = []
                for gi, jc in ((0, fc), (1, ND + fc), (2, 2 * ND + fc)):
                    @blk.gpsimd
                    def _(g):
                        w_stream(g, "C", wst,
                                 wqkvT_p2[jc].rearrange("(dc p) j -> p dc j", p=128))

                    ld_s = wsets["C"]["ld"][(wsets["C"]["i"] - 1) % 2]
                    ld_t = ld_s.n
                    banks = ((0, 1), (2, 3), (4, 5))[gi]

                    @blk.tensor
                    def _(t):
                        t.wait_ge(ld_s.h, ld_t)
                        t.wait_ge(s_attf.h, attf_marks[gi])
                        for dt in range(ND):
                            for ni in range(2):
                                ins = t.matmul(
                                    pb[banks[ni]][:],
                                    wst[(wsets["C"]["i"] - 1) % 2][:, dt * 128:(dt + 1) * 128],
                                    lnT[:, dt * S + half * HALF + ni * 512:
                                        dt * S + half * HALF + (ni + 1) * 512],
                                    start=(dt == 0), stop=(dt == ND - 1))
                        s_pe.inc(ins)
                    wsets["C"]["done"].append((s_pe, s_pe.n))
                    pe_marks.append(s_pe.n)

                pe_q, pe_k, pe_v = pe_marks

                @blk.vector
                def _(v):
                    v.wait_ge(s_pe.h, pe_q)
                    v.tensor_tensor(at3[:, 0:512], pb[0][:], rqb[:, 0:512], op=ALU.mult)
                    ins = v.tensor_tensor(at3[:, 512:1024], pb[1][:], rqb[:, 512:1024],
                                          op=ALU.mult)
                    s_attf.inc(ins)
                    v.wait_ge(s_pe.h, pe_k)
                    v.tensor_tensor(at1[:, 0:512], pb[2][:], rkb[:, 0:512], op=ALU.mult)
                    ins = v.tensor_tensor(at1[:, 512:1024], pb[3][:], rkb[:, 512:1024],
                                          op=ALU.mult)
                    s_attf.inc(ins)
                attf_marks[0] = s_attf.n - 1
                attf_marks[1] = s_attf.n

                @blk.scalar
                def _(e):
                    e.wait_ge(s_attf.h, attf_marks[1])
                    sedge(e, e.activation(at1[:], at1[:], AF.Exp))
                exp_se = s_se.n

                @blk.vector
                def _(v):
                    v.wait_ge(s_se.h, exp_se)
                    v.wait_ge(s_pe.h, pe_v)
                    v.tensor_tensor(at2[:, 0:512], pb[4][:], at1[:, 0:512], op=ALU.mult)
                    ins = v.tensor_tensor(at2[:, 512:1024], pb[5][:], at1[:, 512:1024],
                                          op=ALU.mult)
                    s_attf.inc(ins)
                    v.wait_ge(s_attf.h, s_attf.n)
                    vedge(v, v.tensor_tensor_scan(at4[:], at1[:], zeros[:],
                                                  carry_w[:, fc:fc + 1],
                                                  op0=ALU.add, op1=ALU.add))
                    vedge(v, v.tensor_copy(carry_w[:, fc:fc + 1], at4[:, HALF - 1:HALF]))
                attf_marks[2] = s_attf.n
                wcum_ve = s_ve.n

                @blk.scalar
                def _(e):
                    e.wait_ge(s_attf.h, attf_marks[0])
                    sedge(e, e.activation(at3[:], at3[:], AF.Exp, scale=-1.0))
                    sedge(e, e.activation(at3[:], at3[:], AF.Ln, bias=1.0))
                    sedge(e, e.activation(at3[:], at3[:], AF.Exp, scale=-1.0))
                    e.wait_ge(s_ve.h, wcum_ve)
                    sedge(e, e.activation(at4[:], at4[:], AF.Ln, bias=ceps[:, 2:3]))
                    sedge(e, e.activation(at4[:], at4[:], AF.Exp, scale=-1.0))

                @blk.vector
                def _(v):
                    vedge(v, v.tensor_tensor_scan(at1[:], at2[:], zeros[:],
                                                  carry_kv[:, fc:fc + 1],
                                                  op0=ALU.add, op1=ALU.add))
                    vedge(v, v.tensor_copy(carry_kv[:, fc:fc + 1], at1[:, HALF - 1:HALF]))
                    v.wait_ge(s_se.h, s_se.n)
                    vedge(v, v.tensor_tensor(at1[:], at1[:], at4[:], op=ALU.mult))
                    vedge(v, v.tensor_tensor(
                        yT[:, fc * S + half * HALF: fc * S + (half + 1) * HALF],
                        at1[:], at3[:], op=ALU.mult))

            barrier()

    def emit_p5():
        s_swh = S_("swh")
        s_opv = S_("opv")
        s_adp = adp_sems
        pending = []
        planned = [s_adp[0].n // 16, s_adp[1].n // 16]

        def flush_one():
            if pending:
                pending.pop(0)()

        for half in range(2):
            for sg in range(4):
                swh_base = s_swh.n
                opv_sg_guard = s_opv.n
                for mi in range(16):
                    m = sg * 16 + mi
                    pe_marks = []
                    for gi, src_blk in ((0, wuT_blk), (1, wgT_blk)):
                        @blk.gpsimd
                        def _(g):
                            w_stream(g, "C", wst,
                                     src_blk[m].rearrange("(dc p) j -> p dc j", p=128))
                        if gi == 1:
                            flush_one()

                        ld_s = wsets["C"]["ld"][(wsets["C"]["i"] - 1) % 2]
                        ld_t = ld_s.n
                        banks = (((0, 1), (2, 3)) if mi % 2 == 0 else ((4, 5), (6, 7)))[gi]

                        @blk.tensor
                        def _(t):
                            t.wait_ge(ld_s.h, ld_t)
                            if gi == 0:
                                if mi >= 2:
                                    t.wait_ge(s_swh.h, swh_base + mi - 1)
                                if mi == 0:
                                    t.wait_ge(s_opv.h, opv_sg_guard)
                            for dt in range(ND):
                                for ni in range(2):
                                    ins = t.matmul(
                                        pb[banks[ni]][:],
                                        wst[(wsets["C"]["i"] - 1) % 2][:, dt * 128:(dt + 1) * 128],
                                        yT[:, dt * S + half * HALF + ni * 512:
                                           dt * S + half * HALF + (ni + 1) * 512],
                                        start=(dt == 0), stop=(dt == ND - 1))
                            s_pe.inc(ins)
                        wsets["C"]["done"].append((s_pe, s_pe.n))
                        pe_marks.append(s_pe.n)

                    pe_u, pe_g = pe_marks
                    bu, bg = (((0, 1), (2, 3)) if mi % 2 == 0 else ((4, 5), (6, 7)))

                    @blk.scalar
                    def _(e):
                        e.wait_ge(s_pe.h, pe_g)
                        e.wait_ge(s_swh.h, swh_base + mi)
                        e.activation(silu_t[:, 0:512], pb[bg[0]][:], AF.Silu)
                        sedge(e, e.activation(silu_t[:, 512:1024], pb[bg[1]][:], AF.Silu))

                    @blk.vector
                    def _(v):
                        v.wait_ge(s_se.h, s_se.n)
                        v.wait_ge(s_pe.h, pe_u)
                        col = mi * HALF
                        v.tensor_tensor(h_sg[:, col:col + 512], pb[bu[0]][:],
                                        silu_t[:, 0:512], op=ALU.mult)
                        ins = v.tensor_tensor(h_sg[:, col + 512:col + 1024], pb[bu[1]][:],
                                              silu_t[:, 512:1024], op=ALU.mult)
                        s_swh.inc(ins)
                        v.wait_ge(s_swh.h, s_swh.n)

                for oq in range(4):
                    @blk.gpsimd
                    def _(g):
                        w_stream(g, "D", wob,
                                 woutT[sg * DM:(sg + 1) * DM, oq * 512:(oq + 1) * 512]
                                 .rearrange("(dc p) o -> p dc o", p=128))
                    flush_one()

                    ld_s = wsets["D"]["ld"][(wsets["D"]["i"] - 1) % 2]
                    ld_t = ld_s.n
                    opv_base = s_opv.n
                    ad_base = [planned[0], planned[1]]

                    @blk.tensor
                    def _(t):
                        t.wait_ge(ld_s.h, ld_t)
                        t.wait_ge(s_swh.h, swh_base + 16)
                        for tc in range(NTH):
                            if tc >= 2:
                                t.wait_ge(s_opv.h, opv_base + tc - 1)
                            elif oq == 0 and sg == 0 and half == 0:
                                pass
                            else:
                                t.wait_ge(s_opv.h, opv_base - 2 + tc + 1)
                            for dffc in range(16):
                                ins = t.matmul(
                                    pb[tc % 2][:],
                                    h_sg[:, dffc * HALF + tc * 128: dffc * HALF + tc * 128 + 128],
                                    wob[(wsets["D"]["i"] - 1) % 2][:, dffc * 512:(dffc + 1) * 512],
                                    start=(dffc == 0), stop=(dffc == 15))
                            s_pe.inc(ins)
                    wsets["D"]["done"].append((s_pe, s_pe.n))

                    pe_base = s_pe.n - NTH

                    @blk.vector
                    def _(v):
                        for tc in range(NTH):
                            v.wait_ge(s_pe.h, pe_base + tc + 1)
                            v.wait_ge(s_adp[tc % 2].h,
                                      (ad_base[tc % 2] + tc // 2) * 16)
                            ins = v.tensor_copy(tmpout[tc % 2][:], pb[tc % 2][:])
                            s_opv.inc(ins)

                    def mk_accum(opv_base=opv_base, half=half, sg=sg, oq=oq):
                        def emit():
                            @blk.gpsimd
                            def _(g):
                                for tc in range(NTH):
                                    g.wait_ge(s_opv.h, opv_base + tc + 1)
                                    rows = half * HALF + tc * 128
                                    ins = g.dma_start(
                                        out=xspill[rows:rows + 128,
                                                   oq * 512:(oq + 1) * 512],
                                        in_=tmpout[tc % 2][:], accum_op=ALU.add)
                                    s_adp[tc % 2].inc(ins, 16)
                        return emit

                    pending.append(mk_accum())
                    planned[0] += 4
                    planned[1] += 4

            while pending:
                flush_one()
            barrier()

    if "ffn1" in stages:
        emit_ffn(0, xr1, x_in, do_output=(stages == ("ffn1",)))
    if "attn" in stages:
        emit_a1()
        emit_a2()
        emit_a3()
    if "proj" in stages:
        emit_p5()
    if "ffn2" in stages:
        emit_ffn(1, xr2, xspill, do_output=True)
    elif stages != ("ffn1",):
        @blk.sync
        def _(sy):
            for tc in range(NT):
                ins = sy.dma_start(out=xrt[0][:], in_=xspill[tc * 128:(tc + 1) * 128, :])
                s_x.inc(ins, 16)
                sy.wait_ge(s_x.h, s_x.n)
                ins = sy.dma_start(out=y_out[tc * 128:(tc + 1) * 128, :], in_=xrt[0][:])
                s_od.inc(ins, 16)
                sy.wait_ge(s_od.h, s_od.n)

    @blk.sync
    def _(sy):
        sy.wait_ge(s_od.h, s_od.n)

    blk_ctx.__exit__(None, None, None)
    ctx.close()
    return nc


def np_split_inputs(inputs):
    x = np.asarray(inputs["L0_attn_x"], dtype=np.float32)
    w_qkv = np.asarray(inputs["L0_attn_w_qkv"], dtype=np.float32)
    w_sw = np.asarray(inputs["L0_attn_w_swiglu"], dtype=np.float32)
    w_out = np.asarray(inputs["L0_attn_w_out"], dtype=np.float32)
    gate_w = np.asarray(inputs["L0_ffn_gate_w"], dtype=np.float32)
    exp_w = np.asarray(inputs["L0_ffn_expert_w"], dtype=np.float32)
    exp_b = np.asarray(inputs["L0_ffn_expert_b"], dtype=np.float32)

    wqkvT = np.ascontiguousarray(w_qkv.T)
    shared = {
        "wqkvT_p2": np.ascontiguousarray(wqkvT.reshape(DM, 48, 128).transpose(1, 0, 2)),
        "wqkT_p1": np.ascontiguousarray(wqkvT[:, :2 * DM]),
        "wuT_blk": np.ascontiguousarray(
            w_sw[:DFF].T.reshape(DM, 64, 128).transpose(1, 0, 2)),
        "wgT_blk": np.ascontiguousarray(
            w_sw[DFF:].T.reshape(DM, 64, 128).transpose(1, 0, 2)),
        "woutT": np.ascontiguousarray(w_out.T),
        "gateT": np.ascontiguousarray(gate_w.T),
        "expb": np.ascontiguousarray(exp_b),
        "expT_blk": np.ascontiguousarray(
            exp_w.transpose(0, 2, 1).reshape(NE, DM, 4, 512).transpose(0, 2, 1, 3)),
    }
    return [dict(shared, x=np.ascontiguousarray(x[c])) for c in range(N_CORES)]


def kernel(**inputs):
    in_maps = np_split_inputs(inputs)
    nc = build_program()
    res = run_bass_kernel_spmd(nc, in_maps, core_ids=list(range(N_CORES)))
    return np.stack([np.asarray(r["y"], dtype=np.float32) for r in res.results], axis=0)
